# version 01
# baseline (speedup 1.0000x reference)
import contextlib
import numpy as np
import ml_dtypes
import concourse.bass as bass
import concourse.mybir as mybir
from concourse.bass_utils import run_bass_kernel_spmd

F32 = mybir.dt.float32
BF16 = mybir.dt.bfloat16
AF = mybir.ActivationFunctionType
ALU = mybir.AluOpType
AX = mybir.AxisListType

C = 2048
T = 2048
NCH = C // 128
TB = 512
NTB = T // TB
H = 8192
NHCH = H // 128
EPS = 1e-5
NCORES = 8
NBIS = 44

ENGINES = ("sync", "gpsimd", "vector", "scalar", "tensor")


class Sched:
    def __init__(self, nc):
        self.nc = nc
        self.ops = {e: [] for e in ENGINES}
        self.cnt = {}
        self.sems = {}
        self.guar = {e: {} for e in ENGINES}

    def sem(self, name, stack):
        self.sems[name] = stack.enter_context(self.nc.semaphore(name))
        self.cnt[name] = 0
        self.sem_stack = stack

    def ensure(self, name):
        if name not in self.sems:
            self.sem(name, self.sem_stack)
        return name

    def emit(self, eng, fn, waits=(), inc=None, inc_n=1):
        g = self.guar[eng]
        real_waits = []
        for wt in waits:
            if wt is None:
                continue
            s, v = wt
            if g.get(s, 0) < v:
                real_waits.append((s, v))
                g[s] = v
        tok = None
        if inc is not None:
            self.cnt[inc] += inc_n
            tok = (inc, self.cnt[inc])
            g[inc] = max(g.get(inc, 0), self.cnt[inc])
        self.ops[eng].append((fn, real_waits, (inc, inc_n) if inc else None))
        return tok

    def tok(self, sem):
        return (sem, self.cnt[sem])

    def finalize(self, block):
        for ename in ENGINES:
            ops = self.ops[ename]
            if not ops:
                continue
            deco = getattr(block, ename)

            def section(e, ops=ops):
                sems = self.sems
                for (fn, waits, inc) in ops:
                    for (s, v) in waits:
                        e.wait_ge(sems[s], v)
                    inst = fn(e)
                    if inc is not None:
                        inst.then_inc(sems[inc[0]], inc[1])
            deco(section)



def _tile_lhsT(w):
    K, M = w.shape
    kc, mc = K // 128, M // 128
    s = w.reshape(kc, 128, mc, 128).transpose(2, 1, 0, 3).reshape(mc, 128, K)
    return np.ascontiguousarray(s, dtype=np.float32)


def _cols(v):
    n = v.shape[0] // 128
    return np.ascontiguousarray(v.reshape(n, 128).T, dtype=np.float32)


def _quarters(w, nq=4):
    K, M = w.shape
    kc, mc = K // 128, M // 128
    mq = mc // nq
    s = w.reshape(kc, 128, nq, mq, 128).transpose(2, 1, 3, 0, 4)
    return np.ascontiguousarray(s.reshape(nq, 128, mq * kc * 128),
                                dtype=np.float32)


def make_pe_T(t, d):
    pos = np.arange(t, dtype=np.float64)[:, None]
    i = np.arange(0, d, 2, dtype=np.float64)
    inv_freq = np.exp(-(np.log(10000.0) * i / d))
    ang = pos * inv_freq
    pe = np.zeros((t, d), dtype=np.float64)
    pe[:, 0::2] = np.sin(ang)
    pe[:, 1::2] = np.cos(ang)
    return np.ascontiguousarray(pe.T.astype(np.float32))


def stage_inputs(inputs):
    g = inputs
    shared = {
        "WkS": _tile_lhsT(np.ascontiguousarray(np.asarray(g["L0_attn_a_Wk"]).T)),
        "WvS": _tile_lhsT(np.ascontiguousarray(np.asarray(g["L0_attn_a_Wv"]).T)),
        "WrS": _tile_lhsT(np.ascontiguousarray(np.asarray(g["L0_attn_a_Wr"]).T)),
        "WoS": _tile_lhsT(np.ascontiguousarray(np.asarray(g["L0_attn_a_Wo"]).T)),
        "WcrS": _tile_lhsT(np.ascontiguousarray(np.asarray(g["L0_attn_a_Wcr"]).T)),
        "WckS": _tile_lhsT(np.ascontiguousarray(np.asarray(g["L0_attn_a_Wck"]).T)),
        "WcvQ": _quarters(np.ascontiguousarray(np.asarray(g["L0_attn_a_Wcv"]).T)),
        "wqS": _tile_lhsT(np.asarray(g["L0_attn_b_wq"])),
        "wkvS": _tile_lhsT(np.asarray(g["L0_attn_b_wkv"])),
        "woQ2": _quarters(np.asarray(g["L0_attn_b_wo"]), nq=2),
        "W1S": _tile_lhsT(np.asarray(g["L0_ffn_W1"])),
        "W2Q": _quarters(np.asarray(g["L0_ffn_W2"])),
        "wrr": np.ascontiguousarray(
            np.asarray(g["L0_ffn_W_r"]).reshape(NCH, 128).T.astype(np.float32)),
        "br": np.asarray(g["L0_ffn_b_r"]).reshape(1, 1).astype(np.float32),
        "ln1w_c": _cols(np.asarray(g["L0_attn_a_ln1_w"])),
        "ln1b_c": _cols(np.asarray(g["L0_attn_a_ln1_b"])),
        "ln2w_c": _cols(np.asarray(g["L0_attn_a_ln2_w"])),
        "ln2b_c": _cols(np.asarray(g["L0_attn_a_ln2_b"])),
        "tmk_c": _cols(np.asarray(g["L0_attn_a_tmk"])),
        "tmv_c": _cols(np.asarray(g["L0_attn_a_tmv"])),
        "tmr_c": _cols(np.asarray(g["L0_attn_a_tmr"])),
        "cmk_c": _cols(np.asarray(g["L0_attn_a_cmk"])),
        "cmr_c": _cols(np.asarray(g["L0_attn_a_cmr"])),
        "td_c": _cols(np.asarray(g["L0_attn_a_time_decay"])),
        "tf_c": _cols(np.asarray(g["L0_attn_a_time_first"])),
        "bq_c": _cols(np.asarray(g["L0_attn_b_bq"])),
        "bkv_c": _cols(np.asarray(g["L0_attn_b_bkv"])),
        "b1_c": _cols(np.asarray(g["L0_ffn_b1"])),
        "b2_c": _cols(np.asarray(g["L0_ffn_b2"])),
        "ident": np.eye(128, dtype=np.float32),
        "tri": np.where(np.tril(np.ones((128, 128), bool)), 0.0,
                        -1e38).astype(np.float32),
        "peT": make_pe_T(T, 128),
    }
    x = np.asarray(g["L0_attn_a_x"])
    per_core = []
    for i in range(NCORES):
        m = dict(shared)
        m["xT"] = np.ascontiguousarray(x[i].T.astype(np.float32))
        per_core.append(m)
    return per_core



class Builder:
    def __init__(self, nc, stack, dbg=()):
        self.nc = nc
        self.stack = stack
        self.dbg = set(dbg)
        self.s = Sched(nc)
        for sem in ("ld", "gld", "st", "tE", "vE", "sE", "gE"):
            self.s.sem(sem, stack)
        self.cur_tE = "tE"
        self._phase_n = 0
        self.ld_lanes = [self.s.ensure(f"ldl{i}") for i in range(12)]
        self.st_lanes = [self.s.ensure(f"stl{i}") for i in range(8)]
        self.gld_lanes = [self.s.ensure(f"gldl{i}") for i in range(4)]
        self._ld_i = 0
        self._st_i = 0
        self._gld_i = 0
        self.dram = {}
        self.sb = {}

    def rotate_sems(self):
        self._phase_n += 1
        self.cur_tE = self.s.ensure(f"tE{self._phase_n}")

    def fence(self):
        toks = [self.s.tok(n) for n in self.s.sems]
        incs = {"sync": None, "gpsimd": "gE", "vector": "vE", "scalar": "sE",
                "tensor": self.cur_tE}
        for eng, inc in incs.items():
            self.s.emit(eng, lambda e: e.nop(), waits=toks, inc=inc)

    def new_phase(self):
        self.fence()
        self.rotate_sems()

    def st_all(self):
        return [self.s.tok(n) for n in self.st_lanes]

    def din(self, name, shape, dtype=F32):
        self.dram[name] = self.nc.dram_tensor(name, list(shape), dtype,
                                              kind="ExternalInput").ap()
        return self.dram[name]

    def dout(self, name, shape, dtype=F32):
        self.dram[name] = self.nc.dram_tensor(name, list(shape), dtype,
                                              kind="ExternalOutput").ap()
        return self.dram[name]

    def dscratch(self, name, shape, dtype=BF16):
        kind = "ExternalOutput" if name in self.dbg else "Internal"
        self.dram[name] = self.nc.dram_tensor(name, list(shape), dtype,
                                              kind=kind).ap()
        return self.dram[name]

    def sbuf(self, name, shape, dtype=F32, stack=None, side=None):
        if side is None:
            side = "left" if stack is None else "right"
        t = (stack or self.stack).enter_context(
            self.nc.sbuf_tensor("s_" + name, list(shape), dtype, side=side))
        self.sb[name] = t
        return t

    def psum(self, name, shape, stack=None, dtype=F32):
        t = (stack or self.stack).enter_context(
            self.nc.psum_tensor(name, list(shape), dtype, side="right"))
        self.sb[name] = t
        return t

    def dma(self, q, dst, src, waits=(), sem=None):
        if sem == "st":
            sem = self.st_lanes[self._st_i % len(self.st_lanes)]
            self._st_i += 1
        elif sem is None:
            if q == "gpsimd":
                sem = self.gld_lanes[self._gld_i % len(self.gld_lanes)]
                self._gld_i += 1
            else:
                sem = self.ld_lanes[self._ld_i % len(self.ld_lanes)]
                self._ld_i += 1
        prev = (sem, self.s.cnt[sem])
        return self.s.emit(q, lambda e, d=dst, s=src: e.dma_start(out=d, in_=s),
                           waits=list(waits) + [prev], inc=sem, inc_n=16)


    def _row_math(self, sl, pS1, pS2, mu, rt, rt2, g, mg, double, stats_done):
        s = self.s
        invC = 1.0 / C

        def rv(fn, waits=(), eng="vector", inc="vE"):
            return s.emit(eng, fn, waits=waits, inc=inc)

        rv(lambda e, sl=sl: e.tensor_scalar(mu[0:1, sl], pS1[:], invC, None,
                                            ALU.mult), waits=[stats_done])
        rv(lambda e, sl=sl: e.tensor_scalar(rt[0:1, sl], pS2[:], invC, None,
                                            ALU.mult))
        rv(lambda e, sl=sl: e.tensor_tensor(rt2[0:1, sl], mu[0:1, sl],
                                            mu[0:1, sl], ALU.mult))
        rv(lambda e, sl=sl: e.tensor_tensor(rt[0:1, sl], rt[0:1, sl],
                                            rt2[0:1, sl], ALU.subtract))
        rv(lambda e, sl=sl: e.tensor_scalar(rt[0:1, sl], rt[0:1, sl], EPS,
                                            None, ALU.add))
        sq_tok = rv(lambda e, sl=sl: e.activation(rt2[0:1, sl], rt[0:1, sl],
                                                  AF.Sqrt),
                    waits=[s.tok("vE")], eng="scalar", inc="sE")
        rv(lambda e, sl=sl: e.reciprocal(g[0:1, sl], rt2[0:1, sl]),
           waits=[sq_tok])
        if double:
            rv(lambda e, sl=sl: e.reciprocal(rt2[0:1, sl], rt[0:1, sl]))
            rv(lambda e, sl=sl: e.tensor_scalar(rt[0:1, sl], rt[0:1, sl],
                                                -EPS, None, ALU.add))
            rv(lambda e, sl=sl: e.tensor_tensor(rt2[0:1, sl], rt2[0:1, sl],
                                                rt[0:1, sl], ALU.mult))
            rv(lambda e, sl=sl: e.tensor_scalar(rt2[0:1, sl], rt2[0:1, sl],
                                                EPS, None, ALU.add))
            sq2 = rv(lambda e, sl=sl: e.activation(rt2[0:1, sl], rt2[0:1, sl],
                                                   AF.Sqrt),
                     waits=[s.tok("vE")], eng="scalar", inc="sE")
            rv(lambda e, sl=sl: e.reciprocal(rt2[0:1, sl], rt2[0:1, sl]),
               waits=[sq2])
            rv(lambda e, sl=sl: e.tensor_tensor(g[0:1, sl], g[0:1, sl],
                                                rt2[0:1, sl], ALU.mult))
        return rv(lambda e, sl=sl: e.tensor_tensor(
            mg[0:1, sl], mu[0:1, sl], g[0:1, sl], ALU.mult))

    def stats_rows(self, name, src, double=False, row_stack=None):
        s, sb = self.s, self.sb
        self.new_phase()
        g = self.sbuf(f"g_{name}", [1, T], stack=row_stack,
                      side=None if row_stack else "left")
        mg = self.sbuf(f"mg_{name}", [1, T], stack=row_stack,
                       side=None if row_stack else "left")
        st = contextlib.ExitStack()
        xt = [self.sbuf(f"st{name}x{i}", [128, TB], stack=st) for i in range(2)]
        sqbf = [self.sbuf(f"st{name}sq{i}", [128, TB], BF16, stack=st)
                for i in range(2)]
        xbf = [self.sbuf(f"st{name}xb{i}", [128, TB], BF16, stack=st)
               for i in range(2)]
        rt = self.sbuf(f"st{name}rt", [1, T], stack=st)
        rt2 = self.sbuf(f"st{name}rt2", [1, T], stack=st)
        mu = self.sbuf(f"st{name}mu", [1, T], stack=st)
        pS1 = self.psum(f"st{name}p1", [1, TB], stack=st)
        pS2 = self.psum(f"st{name}p2", [1, TB], stack=st)
        mm_hist = [None, None]
        last_row = None
        src_ready_l = self.st_all()
        for tb in range(NTB):
            sl = slice(tb * TB, (tb + 1) * TB)
            for c in range(NCH):
                i = (tb * NCH + c) % 2
                lt = self.dma("sync", xt[i][:], src[c][:, sl],
                              waits=src_ready_l + [mm_hist[i]])
                guard = [lt, mm_hist[i]]
                sq_t = s.emit("scalar",
                              lambda e, i=i: e.activation(sqbf[i][:], xt[i][:],
                                                          AF.Square),
                              waits=guard, inc="sE")
                xb_t = s.emit("gpsimd",
                              lambda e, i=i: e.tensor_copy(xbf[i][:], xt[i][:]),
                              waits=guard, inc="gE")
                w = [sq_t, xb_t]
                if last_row is not None and c == 0:
                    w.append(last_row)
                s.emit("tensor",
                       lambda e, i=i, c=c: e.matmul(pS1[:], sb["ones_col_bf"][:],
                                                    xbf[i][:], start=(c == 0),
                                                    stop=(c == NCH - 1)),
                       waits=w, inc=self.cur_tE)
                mm_hist[i] = s.emit(
                    "tensor",
                    lambda e, i=i, c=c: e.matmul(pS2[:], sb["ones_col_bf"][:],
                                                 sqbf[i][:], start=(c == 0),
                                                 stop=(c == NCH - 1)),
                    inc=self.cur_tE)
            stats_done = mm_hist[(tb * NCH + NCH - 1) % 2]
            last_row = self._row_math(sl, pS1, pS2, mu, rt, rt2, g, mg,
                                      double, stats_done)
        st.close()
        return g, mg

    def ln_apply_stream(self, src, gname, wcol=None, bcol=None, dst_spill=None,
                        dst_tiles=None, pad=0):
        self.new_phase()
        s, sb = self.s, self.sb
        st = contextlib.ExitStack()
        xt = [self.sbuf(f"la{gname}x{i}", [128, TB], stack=st) for i in range(2)]
        ot = [self.sbuf(f"la{gname}o{i}", [128, TB], BF16, stack=st)
              for i in range(2)]
        tmp = [self.sbuf(f"la{gname}t{i}", [128, TB], stack=st)
               for i in range(2)]
        pB1 = self.psum(f"la{gname}b1", [128, TB], stack=st)
        pB2 = self.psum(f"la{gname}b2", [128, TB], stack=st)
        g, mg = self.sb[f"g_{gname}"], self.sb[f"mg_{gname}"]
        src_ready_l = self.st_all()
        rows_ready = s.tok("vE")
        use_tok = [None, None]
        dma_tok = [None, None]
        bc_free = None
        if dst_spill is not None and pad:
            zpad = self.sbuf(f"la{gname}z", [128, pad], BF16, stack=st)
            zt = s.emit("vector", lambda e: e.memset(zpad[:], 0.0), inc="vE")
            for c in range(NCH):
                self.dma("sync", dst_spill[c][:, 0:pad], zpad[:],
                         waits=[zt], sem="st")
        for tb in range(NTB):
            sl = slice(tb * TB, (tb + 1) * TB)
            s.emit("tensor",
                   lambda e, sl=sl: e.matmul(pB1[:], sb["ones_row"][:],
                                             g[0:1, sl], start=True, stop=True),
                   waits=[rows_ready, bc_free], inc=self.cur_tE)
            bc_tok = s.emit("tensor",
                            lambda e, sl=sl: e.matmul(pB2[:], sb["ones_row"][:],
                                                      mg[0:1, sl], start=True,
                                                      stop=True),
                            inc=self.cur_tE)
            for c in range(NCH):
                i = (tb * NCH + c) % 2
                lt = self.dma("sync", xt[i][:], src[c][:, sl],
                              waits=src_ready_l + [use_tok[i]])
                s.emit("vector",
                       lambda e, i=i: e.tensor_tensor(tmp[i][:], xt[i][:],
                                                      pB1[:], ALU.mult),
                       waits=[lt, bc_tok], inc="vE")
                use_tok[i] = s.tok("vE")
                dst_ap = (ot[i][:] if dst_spill is not None
                          else dst_tiles[c][:, sl])
                if wcol is not None:
                    s.emit("vector",
                           lambda e, i=i: e.tensor_tensor(tmp[i][:], tmp[i][:],
                                                          pB2[:], ALU.subtract),
                           inc="vE")
                    ot_tok = s.emit(
                        "vector",
                        lambda e, i=i, c=c, dst_ap=dst_ap: e.tensor_scalar(
                            dst_ap, tmp[i][:], wcol[:, c:c + 1],
                            bcol[:, c:c + 1], ALU.mult, ALU.add),
                        waits=[dma_tok[i]], inc="vE")
                else:
                    ot_tok = s.emit(
                        "vector",
                        lambda e, i=i, dst_ap=dst_ap: e.tensor_tensor(
                            dst_ap, tmp[i][:], pB2[:], ALU.subtract),
                        waits=[dma_tok[i]], inc="vE")
                use_tok[i] = ot_tok
                if dst_spill is not None:
                    dma_tok[i] = self.dma(
                        "sync",
                        dst_spill[c][:, pad + tb * TB:pad + (tb + 1) * TB],
                        ot[i][:], waits=[ot_tok], sem="st")
                if c == NCH - 1:
                    bc_free = use_tok[i]
        st.close()

    def mix_from_spill(self, src_spill, specs, tag):
        self.new_phase()
        s, sb = self.s, self.sb
        st = contextlib.ExitStack()
        at = [self.sbuf(f"mx{tag}a{i}", [128, TB + 1], BF16, stack=st)
              for i in range(2)]
        d1 = [self.sbuf(f"mx{tag}d{i}", [128, TB], BF16, stack=st)
              for i in range(2)]
        src_ready_l = self.st_all()
        use_tok = [None, None]
        for c in range(NCH):
            for tb in range(NTB):
                i = (c * NTB + tb) % 2
                lt = self.dma("sync", at[i][:],
                              src_spill[c][:, 1 + tb * TB:2 + (tb + 1) * TB],
                              waits=src_ready_l + [use_tok[i]])
                s.emit("vector",
                       lambda e, i=i: e.tensor_tensor(
                           d1[i][:], at[i][:, 1:TB + 1], at[i][:, 0:TB],
                           ALU.subtract),
                       waits=[lt], inc="vE")
                for (dst, mcol) in specs:
                    use_tok[i] = s.emit(
                        "vector",
                        lambda e, i=i, c=c, tb=tb, dst=dst, mcol=mcol:
                        e.scalar_tensor_tensor(
                            dst[c][:, tb * TB:(tb + 1) * TB], d1[i][:],
                            sb[mcol][:, c:c + 1], at[i][:, 0:TB],
                            ALU.mult, ALU.add),
                        inc="vE")
        st.close()
        return s.tok("vE")

    def proj_stream(self, rhs_tiles, WS, evac, n_out=NCH, tag="pj"):
        self.new_phase()
        s, d = self.s, self.dram
        st = contextlib.ExitStack()
        wstr = [self.sbuf(f"{tag}w{i}", [128, C], BF16, stack=st)
                for i in range(2)]
        pP = [self.psum(f"{tag}p{i}", [128, T], stack=st) for i in range(2)]
        rhs_ready = s.tok("vE")
        strip_tok = [None, None]
        psum_free = [None, None]
        for dd in range(2):
            strip_tok[dd] = self.dma("gpsimd", wstr[dd][:], d[WS][dd],
                                     waits=[rhs_ready])
        for dd in range(n_out):
            par = dd % 2
            mmw = [strip_tok[par], rhs_ready, psum_free[par]]
            for tb in range(NTB):
                so = tb * TB
                for kc in range(NCH):
                    s.emit("tensor",
                           lambda e, par=par, so=so, kc=kc: e.matmul(
                               pP[par][:, so:so + TB],
                               wstr[par][:, kc * 128:(kc + 1) * 128],
                               rhs_tiles[kc][:, so:so + TB],
                               start=(kc == 0), stop=(kc == NCH - 1)),
                           waits=mmw if kc == 0 and tb == 0 else (),
                           inc=self.cur_tE)
            mm_done = s.tok(self.cur_tE)
            psum_free[par] = evac(dd, par, pP, mm_done, psum_free[par])
            if dd + 2 < n_out:
                strip_tok[par] = self.dma("gpsimd", wstr[par][:], d[WS][dd + 2],
                                          waits=[mm_done])
        st.close()


    def build_all(self, upto="OUT"):
        d = self.dram
        self.din("xT", [C, T])
        for n, sh in [("WkS", [NCH, 128, C]), ("WvS", [NCH, 128, C]),
                      ("WrS", [NCH, 128, C]), ("WoS", [NCH, 128, C]),
                      ("WcrS", [NCH, 128, C]), ("WckS", [NHCH, 128, C]),
                      ("WcvQ", [4, 128, H * C // 4 // 128]),
                      ("wqS", [NCH, 128, C]), ("wkvS", [8, 128, C]),
                      ("woQ2", [2, 128, NCH * 8 * 128]), ("W1S", [NHCH, 128, C]),
                      ("W2Q", [4, 128, H * C // 4 // 128]),
                      ("wrr", [128, NCH]), ("br", [1, 1])]:
            self.din(n, sh)
        for n in ("ln1w_c", "ln1b_c", "ln2w_c", "ln2b_c", "tmk_c", "tmv_c",
                  "tmr_c", "cmk_c", "cmr_c", "td_c", "tf_c", "bq_c", "b2_c"):
            self.din(n, [128, NCH])
        self.din("bkv_c", [128, 8])
        self.din("b1_c", [128, NHCH])
        self.din("ident", [128, 128])
        self.din("tri", [128, 128])
        self.din("peT", [128, T])
        self.dout("yT", [C, T])
        self.dscratch("sp_ain", [NCH, 128, T + 2])
        self.dscratch("sp_sr", [NCH, 128, T])
        self.dscratch("sp_z", [NCH, 128, T])
        self.dscratch("sp_n1", [NCH, 128, T + 2])
        self.dscratch("sp_scr", [NCH, 128, T])
        self.dscratch("sp_h", [NHCH, 128, T])
        self.dscratch("sp_xc", [NCH, 128, T], dtype=F32)
        self.dscratch("sp_row", [1, T], dtype=F32)
        self.dscratch("sp_col", [128, 1], dtype=F32)

        for n in ("ln1w_c", "ln1b_c", "ln2w_c", "ln2b_c", "tmk_c", "tmv_c",
                  "tmr_c", "cmk_c", "cmr_c", "td_c", "tf_c", "bq_c", "b2_c"):
            self.sbuf(n, [128, NCH])
        self.sbuf("bkv_c", [128, 8])
        self.sbuf("b1_c", [128, NHCH])
        self.sbuf("d_col", [128, NCH])
        self.sbuf("eu_col", [128, NCH])
        self.sbuf("ident_bf", [128, 128], BF16)
        self.sbuf("tri", [128, 128])
        self.sbuf("ones_row", [1, 128])
        self.sbuf("ones_col_bf", [128, 1], BF16)
        self.sbuf("peT", [128, T])

        sp = self.dram
        self.phase_P0()
        self.phase_P1()
        if upto == "P1":
            return
        self.phase_proj_sig("sp_ain", "tmr_c", "WrS", "sp_sr", tag="sr")
        if upto == "SR":
            return
        self.phase_P3()
        if upto == "Z":
            return
        self.phase_A3()
        self.st_mu.close()
        if upto == "A3":
            return
        st2 = contextlib.ExitStack()
        self.stats_rows("ln2", sp["sp_xc"], row_stack=st2)
        self.ln_apply_stream(sp["sp_xc"], "ln2", self.sb["ln2w_c"],
                             self.sb["ln2b_c"], dst_spill=sp["sp_n1"], pad=2)
        st2.close()
        if upto == "N1":
            return
        self.phase_A4b()
        if upto == "A4B":
            return
        self.phase_proj_sig("sp_n1", "cmr_c", "WcrS", "sp_scr", tag="scr")
        if upto == "SCR":
            return
        self.phase_A4c2()
        if upto == "A4C":
            return
        self.phase_A4d()
        if upto == "A4":
            return
        self.phase_F("ln3", out_final=False)
        if upto == "F1":
            return
        self.phase_B()
        if upto == "B":
            return
        self.phase_F("ln5", out_final=True)

    def phase_F(self, name, out_final):
        s, sb, d = self.s, self.sb, self.dram
        stG = contextlib.ExitStack()
        gbc = self.sbuf(f"{name}gbc", [128, T], stack=stG)
        invbc = self.sbuf(f"{name}ibc", [128, T], stack=stG)

        stN = contextlib.ExitStack()
        self.stats_rows(name, d["sp_xc"], row_stack=stN)
        n3 = [self.sbuf(f"{name}n{c}", [128, T], BF16, stack=stN)
              for c in range(NCH)]
        self.ln_apply_stream(d["sp_xc"], name, dst_tiles=n3)
        n3_done = s.tok("vE")

        self.new_phase()
        wrb = self.sbuf(f"{name}wrb", [128, NCH], BF16, stack=stN)
        wrf = self.sbuf(f"{name}wrf", [128, NCH], stack=stN)
        brs = self.sbuf(f"{name}brs", [1, 1], stack=stN)
        w_row = self.sbuf(f"{name}wrow", [1, T], stack=stN)
        wrf_t = self.dma("sync", wrf[:], d["wrr"])
        wrb_t = s.emit("vector", lambda e: e.tensor_copy(wrb[:], wrf[:]),
                       waits=[wrf_t], inc="vE")
        brs_t = self.dma("sync", brs[:], d["br"])
        if "fdbg" in self.dbg:
            db2 = self.dout(f"fdbg2_{name}", [128, T], BF16)
            db3 = self.dout(f"fdbg3_{name}", [128, NCH], BF16)
            self.dma("sync", db2[:, :], n3[0][:, :], waits=[n3_done], sem="st")
            self.dma("sync", db3[:, :], wrb[:, :], waits=[wrb_t], sem="st")
        stR = contextlib.ExitStack()
        pW = self.psum(f"{name}pw", [1, T], stack=stR)
        for tb in range(NTB):
            sl = slice(tb * TB, (tb + 1) * TB)
            for kc in range(NCH):
                s.emit("tensor",
                       lambda e, sl=sl, kc=kc: e.matmul(
                           pW[0:1, sl], wrb[:, kc:kc + 1], n3[kc][:, sl],
                           start=(kc == 0), stop=(kc == NCH - 1)),
                       waits=[wrb_t, n3_done] if kc == 0 and tb == 0 else (),
                       inc=self.cur_tE)
        mmw_t = s.tok(self.cur_tE)
        s.emit("vector",
               lambda e: e.tensor_scalar(w_row[0:1, :], pW[0:1, :],
                                         brs[0:1, 0:1], None, ALU.add),
               waits=[mmw_t, brs_t], inc="vE")

        wcol = self.sbuf(f"{name}wcol", [128, NCH], stack=stN)
        amx = self.sbuf(f"{name}amx", [128, 1], stack=stN)
        amxr = self.sbuf(f"{name}amxr", [1, 128], stack=stN)
        bis = self.sbuf(f"{name}bis", [1, 8], stack=stN)
        midc = self.sbuf(f"{name}midc", [128, 1], stack=stN)
        gtb = self.sbuf(f"{name}gtb", [128, NCH], BF16, stack=stN)
        pM = self.psum(f"{name}pm", [128, 1], stack=stR)
        pC = self.psum(f"{name}pc", [1, NCH], stack=stR)
        wr_st = self.dma("sync", d["sp_row"], w_row[0:1, :],
                         waits=[s.tok("vE")], sem="st")
        wc_t = self.dma("sync", wcol[:],
                        d["sp_row"].rearrange("a (p s) -> (a p) s", p=128),
                        waits=[wr_st])
        LO, HI, MID, CNT, GE, T1 = (slice(0, 1), slice(1, 2), slice(2, 3),
                                    slice(3, 4), slice(4, 5), slice(5, 6))

        def bv(fn, waits=()):
            tok = s.emit("vector", fn, waits=waits, inc="vE")
            s.emit("vector", lambda e: e.drain(), inc="vE")
            return tok

        bv(lambda e: e.reduce_max(amx[:, 0:1], wcol[:], AX.X,
                                  apply_absolute_value=True), waits=[wc_t])
        ax_st = self.dma("sync", d["sp_col"], amx[:, 0:1],
                         waits=[s.tok("vE")], sem="st")
        ax_t = self.dma("sync", amxr[:],
                        d["sp_col"].rearrange("p a -> a p"),
                        waits=[ax_st])
        bv(lambda e: e.reduce_max(bis[0:1, HI], amxr[0:1, :], AX.X),
           waits=[ax_t])
        bv(lambda e: e.tensor_scalar(bis[0:1, HI], bis[0:1, HI], 1.0, None,
                                     ALU.add))
        bv(lambda e: e.tensor_scalar(bis[0:1, LO], bis[0:1, HI], -1.0, None,
                                     ALU.mult))
        for it in range(NBIS):
            bv(lambda e: e.tensor_tensor(bis[0:1, MID], bis[0:1, LO],
                                         bis[0:1, HI], ALU.add))
            bv(lambda e: e.tensor_scalar(bis[0:1, MID], bis[0:1, MID], 0.5,
                                         None, ALU.mult))
            s.emit("tensor",
                   lambda e: e.matmul(pM[:], sb["ones_row"][:], bis[0:1, MID],
                                      start=True, stop=True),
                   waits=[s.tok("vE")], inc=self.cur_tE)
            bv(lambda e: e.tensor_copy(midc[:], pM[:]), waits=[s.tok(self.cur_tE)])
            bv(lambda e: e.tensor_scalar(gtb[:], wcol[:], midc[:, 0:1], None,
                                         ALU.is_gt))
            s.emit("tensor",
                   lambda e: e.matmul(pC[:], sb["ones_col_bf"][:], gtb[:],
                                      start=True, stop=True),
                   waits=[s.tok("vE")], inc=self.cur_tE)
            bv(lambda e: e.reduce_sum(bis[0:1, CNT], pC[0:1, :], AX.X),
               waits=[s.tok(self.cur_tE)])
            bv(lambda e: e.tensor_scalar(bis[0:1, GE], bis[0:1, CNT], 511.5,
                                         None, ALU.is_ge))
            bv(lambda e: e.tensor_tensor(bis[0:1, T1], bis[0:1, MID],
                                         bis[0:1, LO], ALU.subtract))
            bv(lambda e: e.tensor_tensor(bis[0:1, T1], bis[0:1, T1],
                                         bis[0:1, GE], ALU.mult))
            bv(lambda e: e.tensor_tensor(bis[0:1, LO], bis[0:1, LO],
                                         bis[0:1, T1], ALU.add))
            bv(lambda e: e.tensor_tensor(bis[0:1, T1], bis[0:1, HI],
                                         bis[0:1, MID], ALU.subtract))
            bv(lambda e: e.tensor_tensor(bis[0:1, T1], bis[0:1, T1],
                                         bis[0:1, GE], ALU.mult))
            bv(lambda e: e.tensor_tensor(bis[0:1, HI], bis[0:1, MID],
                                         bis[0:1, T1], ALU.add))
            if "ftrace" in self.dbg and it < 16:
                dbt = self.dram.get(f"ftrace_{name}")
                if dbt is None:
                    dbt = self.dout(f"ftrace_{name}", [16, 8])
                self.dma("sync", dbt[it:it + 1, :], bis[0:1, :],
                         waits=[s.tok("vE")], sem="st")
        gw_row = self.sbuf(f"{name}gw", [1, T], stack=stN)
        inv_row = self.sbuf(f"{name}inv", [1, T], stack=stN)
        bv(lambda e: e.tensor_scalar(gw_row[0:1, :], w_row[0:1, :],
                                     bis[0:1, HI], None, ALU.is_gt))
        bv(lambda e: e.tensor_tensor(gw_row[0:1, :], gw_row[0:1, :],
                                     w_row[0:1, :], ALU.mult))
        bv(lambda e: e.tensor_scalar(inv_row[0:1, :], w_row[0:1, :],
                                     bis[0:1, HI], None, ALU.is_le))
        rows_t = s.tok("vE")
        if "fdbg" in self.dbg:
            db = self.dout(f"fdbg_{name}", [4, T])
            self.dma("sync", db[0:1, :], w_row[0:1, :], waits=[rows_t], sem="st")
            self.dma("sync", db[1:2, 0:8], bis[0:1, :], sem="st")
            self.dma("sync", db[2:3, :], gw_row[0:1, :], sem="st")
            self.dma("sync", db[3:4, :], inv_row[0:1, :], sem="st")
        pB = self.psum(f"{name}pb", [128, TB], stack=stR)
        prev = None
        for (dstt, row) in ((gbc, gw_row), (invbc, inv_row)):
            for tb in range(NTB):
                sl = slice(tb * TB, (tb + 1) * TB)
                s.emit("tensor",
                       lambda e, sl=sl, row=row: e.matmul(
                           pB[:], sb["ones_row"][:], row[0:1, sl],
                           start=True, stop=True),
                       waits=[rows_t, prev], inc=self.cur_tE)
                mmt = s.tok(self.cur_tE)
                prev = s.emit("vector",
                              lambda e, sl=sl, dstt=dstt: e.tensor_copy(
                                  dstt[:, sl], pB[:]),
                              waits=[mmt], inc="vE")
        stR.close()
        bc_done = prev

        self.new_phase()
        stU = contextlib.ExitStack()
        xct = [self.sbuf(f"{name}ux{i}", [128, TB], stack=stU)
               for i in range(2)]
        tmu = [self.sbuf(f"{name}ut{i}", [128, TB], stack=stU)
               for i in range(2)]
        use = [None, None]
        for c in range(NCH):
            for tb in range(NTB):
                sl = slice(tb * TB, (tb + 1) * TB)
                i = (c * NTB + tb) % 2
                xl = self.dma("sync", xct[i][:], d["sp_xc"][c][:, sl],
                              waits=[use[i]])
                s.emit("vector",
                       lambda e, i=i, c=c, sl=sl: e.tensor_tensor(
                           tmu[i][:], n3[c][:, sl], invbc[:, sl], ALU.mult),
                       waits=[bc_done, xl], inc="vE")
                t = s.emit("vector",
                           lambda e, i=i: e.tensor_tensor(
                               xct[i][:], xct[i][:], tmu[i][:], ALU.add),
                           inc="vE")
                use[i] = self.dma("sync", d["sp_xc"][c][:, sl], xct[i][:],
                                  waits=[t], sem="st")
        stU.close()

        stM = contextlib.ExitStack()
        self.mlp_up(n3, "W1S", act="gelu", st=stM, tag=name + "u")
        stM.close()
        stN.close()
        dst = None
        if out_final:
            dst = [d["yT"][dd * 128:(dd + 1) * 128, :] for dd in range(NCH)]
        self.mlp_down("W2Q", tag=name + "d", gbc=gbc, b2col=sb["b2_c"],
                      dst=dst)
        stG.close()

    def phase_B(self):
        s, sb, d = self.s, self.sb, self.dram
        stN4 = contextlib.ExitStack()
        n4 = [self.sbuf(f"bn{c}", [128, T], BF16, stack=stN4, side="left")
              for c in range(NCH)]
        stRows = contextlib.ExitStack()
        self.stats_rows("ln4", d["sp_xc"], row_stack=stRows)
        self.ln_apply_stream(d["sp_xc"], "ln4", dst_tiles=n4)
        stRows.close()
        stQ = contextlib.ExitStack()
        q = [self.sbuf(f"bq{c}", [128, T], BF16, stack=stQ)
             for c in range(NCH)]
        kt = [self.sbuf(f"bk{g}", [128, T], BF16, stack=stQ) for g in range(4)]
        vT = [self.sbuf(f"bv{g}", [128, T], BF16, stack=stQ) for g in range(4)]

        def evq(dd, par, pP, mm_done, prev):
            last = None
            for tb in range(NTB):
                sl = slice(tb * TB, (tb + 1) * TB)
                last = s.emit(
                    "vector",
                    lambda e, dd=dd, par=par, sl=sl, pP=pP:
                    e.scalar_tensor_tensor(
                        q[dd][:, sl], pP[par][:, sl], sb["bq_c"][:, dd:dd + 1],
                        sb["peT"][:, sl], ALU.add, ALU.add),
                    waits=[mm_done], inc="vE")
            return last

        self.proj_stream(n4, "wqS", evq, tag="bq")

        self.new_phase()
        stKV = contextlib.ExitStack()
        wstr = [self.sbuf(f"bkw{i}", [128, C], BF16, stack=stKV)
                for i in range(2)]
        vtmp = self.sbuf("bvt", [128, T], BF16, stack=stKV)
        pKV = self.psum("bpkv", [128, T], stack=stKV)
        pTr = self.psum("bptr", [128, 128], stack=stKV, dtype=BF16)
        n4_done = s.tok("vE")
        strip_tok = [None, None]
        strip_tok[0] = self.dma("gpsimd", wstr[0][:], d["wkvS"][0],
                                waits=[n4_done])
        strip_tok[1] = self.dma("gpsimd", wstr[1][:], d["wkvS"][1])
        psum_free = None
        tr_free = None
        for oc in range(8):
            g, is_v = divmod(oc, 2)
            par = oc % 2
            mmw = [strip_tok[par], n4_done, psum_free]
            for tb in range(NTB):
                so = tb * TB
                for kc in range(NCH):
                    s.emit("tensor",
                           lambda e, par=par, so=so, kc=kc: e.matmul(
                               pKV[:, so:so + TB],
                               wstr[par][:, kc * 128:(kc + 1) * 128],
                               n4[kc][:, so:so + TB],
                               start=(kc == 0), stop=(kc == NCH - 1)),
                           waits=mmw if kc == 0 and tb == 0 else (),
                           inc=self.cur_tE)
            mm_done = s.tok(self.cur_tE)
            if not is_v:
                last = None
                for tb in range(NTB):
                    sl = slice(tb * TB, (tb + 1) * TB)
                    last = s.emit(
                        "vector",
                        lambda e, g=g, oc=oc, sl=sl: e.scalar_tensor_tensor(
                            kt[g][:, sl], pKV[:, sl], sb["bkv_c"][:, oc:oc + 1],
                            sb["peT"][:, sl], ALU.add, ALU.add),
                        waits=[mm_done], inc="vE")
                psum_free = last
            else:
                vv = s.emit("vector",
                            lambda e, oc=oc: e.tensor_scalar(
                                vtmp[:], pKV[:], sb["bkv_c"][:, oc:oc + 1],
                                None, ALU.add),
                            waits=[mm_done], inc="vE")
                psum_free = vv
                for tt in range(NCH):
                    s.emit("tensor",
                           lambda e, tt=tt: e.transpose(
                               pTr[:], vtmp[:, tt * 128:(tt + 1) * 128],
                               sb["ident_bf"][:]),
                           waits=[vv, tr_free], inc=self.cur_tE)
                    trt = s.tok(self.cur_tE)
                    tr_free = s.emit(
                        "vector",
                        lambda e, g=g, tt=tt: e.tensor_copy(
                            vT[g][:, tt * 128:(tt + 1) * 128], pTr[:]),
                        waits=[trt], inc="vE")
            if oc + 2 < 8:
                strip_tok[par] = self.dma("gpsimd", wstr[par][:],
                                          d["wkvS"][oc + 2], waits=[mm_done])
        stKV.close()
        stN4.close()

        self.new_phase()
        stA = contextlib.ExitStack()
        probs = self.sbuf("bpr", [128, T], BF16, stack=stA)
        pT = self.sbuf("bpt", [128, T], BF16, stack=stA)
        aot = [self.sbuf(f"bao{i}", [128, T], BF16, stack=stA)
               for i in range(2)]
        mx = self.sbuf("bmx", [128, 1], stack=stA)
        nmx = self.sbuf("bnmx", [128, 1], stack=stA)
        lrow = self.sbuf("blr", [128, 1], stack=stA)
        linv = self.sbuf("bli", [128, 1], stack=stA)
        pSc = self.psum("bpsc", [128, T], stack=stA)
        pTr2 = self.psum("bptr2", [128, 128], stack=stA, dtype=BF16)
        pPV = self.psum("bppv", [128, 128], stack=stA)
        ISD = 0.08838834764831845
        qk_done = s.tok("vE")
        sc_free = None
        tr2_free = None
        pv_free = None
        ao_dma = [None, None]
        for h in range(NCH):
            g = h // 4
            hp = h % 2
            for r in range(NCH):
                nk = (r + 1) * 128
                mmw = [qk_done, sc_free]
                for mo in range(0, nk, TB):
                    mw = min(TB, nk - mo)
                    s.emit("tensor",
                           lambda e, h=h, g=g, r=r, mo=mo, mw=mw: e.matmul(
                               pSc[:, mo:mo + mw],
                               q[h][:, r * 128:(r + 1) * 128],
                               kt[g][:, mo:mo + mw], start=True, stop=True),
                           waits=mmw if mo == 0 else (), inc=self.cur_tE)
                mm_done = s.tok(self.cur_tE)
                mk = s.emit("vector",
                            lambda e, r=r: e.tensor_tensor(
                                pSc[:, r * 128:(r + 1) * 128],
                                pSc[:, r * 128:(r + 1) * 128],
                                sb["tri"][:], ALU.add),
                            waits=[mm_done], inc="vE")
                s.emit("vector",
                       lambda e, nk=nk: e.reduce_max(mx[:, 0:1],
                                                     pSc[:, 0:nk], AX.X),
                       inc="vE")
                s.emit("vector", lambda e: e.drain(), inc="vE")
                nm = s.emit("vector",
                            lambda e: e.tensor_scalar(nmx[:, 0:1], mx[:, 0:1],
                                                      -ISD, None, ALU.mult),
                            inc="vE")
                s.emit("vector", lambda e: e.drain(), inc="vE")
                ex = s.emit("scalar",
                            lambda e, nk=nk: e.activation(
                                probs[:, 0:nk], pSc[:, 0:nk], AF.Exp,
                                bias=nmx[:, 0:1], scale=ISD,
                                accum_out=lrow[:, 0:1]),
                            waits=[nm], inc="sE")
                sc_free = ex
                s.emit("vector",
                       lambda e: e.reciprocal(linv[:, 0:1], lrow[:, 0:1]),
                       waits=[ex], inc="vE")
                s.emit("vector", lambda e: e.drain(), inc="vE")
                pn = s.emit("vector",
                            lambda e, nk=nk: e.tensor_scalar(
                                probs[:, 0:nk], probs[:, 0:nk], linv[:, 0:1],
                                None, ALU.mult),
                            inc="vE")
                for tt in range(r + 1):
                    s.emit("tensor",
                           lambda e, tt=tt: e.transpose(
                               pTr2[:], probs[:, tt * 128:(tt + 1) * 128],
                               sb["ident_bf"][:]),
                           waits=[pn, tr2_free], inc=self.cur_tE)
                    trt = s.tok(self.cur_tE)
                    tr2_free = s.emit(
                        "vector",
                        lambda e, tt=tt: e.tensor_copy(
                            pT[:, tt * 128:(tt + 1) * 128], pTr2[:]),
                        waits=[trt], inc="vE")
                pt_done = tr2_free
                for tt in range(r + 1):
                    s.emit("tensor",
                           lambda e, g=g, tt=tt, r=r: e.matmul(
                               pPV[:], vT[g][:, tt * 128:(tt + 1) * 128],
                               pT[:, tt * 128:(tt + 1) * 128],
                               start=(tt == 0), stop=(tt == r)),
                           waits=[pt_done, pv_free] if tt == 0 else (),
                           inc=self.cur_tE)
                pv_done = s.tok(self.cur_tE)
                pv_free = s.emit("vector",
                                 lambda e, hp=hp, r=r: e.tensor_copy(
                                     aot[hp][:, r * 128:(r + 1) * 128],
                                     pPV[:]),
                                 waits=[pv_done, ao_dma[hp]], inc="vE")
            ao_dma[hp] = self.dma("sync", d["sp_z"][h], aot[hp][:],
                                  waits=[pv_free], sem="st")
        stA.close()

        self.new_phase()
        stB4 = contextlib.ExitStack()
        KWo = NCH * 128
        woh = self.sbuf("b4w", [128, 8 * KWo], BF16, stack=stB4)
        atl = [self.sbuf(f"b4a{i}", [128, TB], BF16, stack=stB4)
               for i in range(2)]
        xct = [self.sbuf(f"b4x{i}", [128, TB], stack=stB4) for i in range(2)]
        pB4 = self.psum("b4p", [128, 8 * TB], stack=stB4)
        ao_ready_l = self.st_all()
        xc_tok = [None, None]
        at_tok = [None, None]
        ev_done = None
        for half in range(2):
            wo_t = self.dma("gpsimd", woh[:], d["woQ2"][half],
                            waits=[ev_done])
            for tb in range(NTB):
                sl = slice(tb * TB, (tb + 1) * TB)
                for kc in range(NCH):
                    i = kc % 2
                    lt = self.dma("sync", atl[i][:], d["sp_z"][kc][:, sl],
                                  waits=ao_ready_l + [at_tok[i]])
                    for j in range(8):
                        s.emit("tensor",
                               lambda e, j=j, kc=kc, i=i: e.matmul(
                                   pB4[:, j * TB:(j + 1) * TB],
                                   woh[:, j * KWo + kc * 128:
                                       j * KWo + (kc + 1) * 128],
                                   atl[i][:], start=(kc == 0),
                                   stop=(kc == NCH - 1)),
                               waits=([wo_t, lt, ev_done]
                                      if kc == 0 and j == 0 else [lt]),
                               inc=self.cur_tE)
                        if j == 7:
                            at_tok[i] = s.tok(self.cur_tE)
                mm_done = s.tok(self.cur_tE)
                for j in range(8):
                    dd = half * 8 + j
                    i = j % 2
                    xl = self.dma("sync", xct[i][:], d["sp_xc"][dd][:, sl],
                                  waits=ao_ready_l + [xc_tok[i]])
                    ev_done = s.emit(
                        "vector",
                        lambda e, i=i, j=j: e.tensor_tensor(
                            xct[i][:], pB4[:, j * TB:(j + 1) * TB],
                            xct[i][:], ALU.add),
                        waits=[xl, mm_done], inc="vE")
                    xc_tok[i] = self.dma("sync", d["sp_xc"][dd][:, sl],
                                         xct[i][:], waits=[ev_done], sem="st")
        stB4.close()
        stQ.close()

    def phase_P0(self):
        s, d, sb = self.s, self.dram, self.sb
        for n in ("ln1w_c", "ln1b_c", "ln2w_c", "ln2b_c", "tmk_c", "tmv_c",
                  "tmr_c", "cmk_c", "cmr_c", "td_c", "tf_c", "bq_c", "b2_c",
                  "bkv_c", "b1_c", "tri"):
            self.dma("sync", sb[n][:], d[n])
        self.dma("gpsimd", sb["ident_bf"][:], d["ident"])
        self.dma("sync", sb["peT"][:], d["peT"])
        ld_all_l = [s.tok(l) for l in self.ld_lanes] + \
                   [s.tok(l) for l in self.gld_lanes]
        s.emit("vector", lambda e: e.memset(sb["ones_row"][:], 1.0), inc="vE")
        s.emit("vector", lambda e: e.memset(sb["ones_col_bf"][:], 1.0), inc="vE")
        s.emit("scalar", lambda e: e.activation(sb["d_col"][:], sb["td_c"][:],
                                                AF.Exp),
               waits=ld_all_l, inc="sE")
        s.emit("scalar", lambda e: e.activation(sb["d_col"][:], sb["d_col"][:],
                                                AF.Exp, scale=-1.0), inc="sE")
        s.emit("scalar", lambda e: e.activation(sb["eu_col"][:], sb["tf_c"][:],
                                                AF.Exp), inc="sE")

    def phase_P1(self):
        s, d, sb = self.s, self.dram, self.sb
        xTd = [d["xT"][c * 128:(c + 1) * 128, :] for c in range(NCH)]
        self.st_mu = contextlib.ExitStack()
        self.sbuf("mu0", [1, T], stack=self.st_mu, side="left")
        self.sbuf("r1", [1, T], stack=self.st_mu, side="left")
        st0 = contextlib.ExitStack()
        self.stats_rows_ln0(xTd, st0)
        self.ln_apply_stream(xTd, "ln0", sb["ln1w_c"], sb["ln1b_c"],
                             dst_spill=self.dram["sp_ain"], pad=2)
        st0.close()

    def stats_rows_ln0(self, src, row_stack):
        s, sb = self.s, self.sb
        st = contextlib.ExitStack()
        xt = [self.sbuf(f"p1x{i}", [128, TB], stack=st) for i in range(2)]
        sqbf = [self.sbuf(f"p1sq{i}", [128, TB], BF16, stack=st)
                for i in range(2)]
        xbf = [self.sbuf(f"p1xb{i}", [128, TB], BF16, stack=st)
               for i in range(2)]
        rt = self.sbuf("p1rt", [1, T], stack=st)
        rt2 = self.sbuf("p1rt2", [1, T], stack=st)
        pS1 = self.psum("p1s1", [1, TB], stack=st)
        pS2 = self.psum("p1s2", [1, TB], stack=st)
        g = self.sbuf("g_ln0", [1, T], stack=row_stack, side="left")
        mg = self.sbuf("mg_ln0", [1, T], stack=row_stack, side="left")
        mu0, r1 = sb["mu0"], sb["r1"]
        mm_hist = [None, None]
        last_row = None
        for tb in range(NTB):
            sl = slice(tb * TB, (tb + 1) * TB)
            for c in range(NCH):
                i = (tb * NCH + c) % 2
                lt = self.dma("sync", xt[i][:], src[c][:, sl],
                              waits=[mm_hist[i]])
                guard = [lt, mm_hist[i]]
                sq_t = s.emit("scalar",
                              lambda e, i=i: e.activation(sqbf[i][:], xt[i][:],
                                                          AF.Square),
                              waits=guard, inc="sE")
                xb_t = s.emit("gpsimd",
                              lambda e, i=i: e.tensor_copy(xbf[i][:], xt[i][:]),
                              waits=guard, inc="gE")
                w = [sq_t, xb_t]
                if last_row is not None and c == 0:
                    w.append(last_row)
                s.emit("tensor",
                       lambda e, i=i, c=c: e.matmul(pS1[:], sb["ones_col_bf"][:],
                                                    xbf[i][:], start=(c == 0),
                                                    stop=(c == NCH - 1)),
                       waits=w, inc=self.cur_tE)
                mm_hist[i] = s.emit(
                    "tensor",
                    lambda e, i=i, c=c: e.matmul(pS2[:], sb["ones_col_bf"][:],
                                                 sqbf[i][:], start=(c == 0),
                                                 stop=(c == NCH - 1)),
                    inc=self.cur_tE)
            stats_done = mm_hist[(tb * NCH + NCH - 1) % 2]
            last_row = self._row_math(sl, pS1, pS2, mu0, rt, rt2, g, mg,
                                      True, stats_done)
            t = s.emit("vector",
                       lambda e, sl=sl: e.tensor_scalar(rt[0:1, sl], rt[0:1, sl],
                                                        EPS, None, ALU.add),
                       inc="vE")
            sq = s.emit("scalar",
                        lambda e, sl=sl: e.activation(rt2[0:1, sl], rt[0:1, sl],
                                                      AF.Sqrt),
                        waits=[t], inc="sE")
            last_row = s.emit("vector",
                              lambda e, sl=sl: e.reciprocal(r1[0:1, sl],
                                                            rt2[0:1, sl]),
                              waits=[sq], inc="vE")
        st.close()

    def phase_proj_sig(self, src_name, mix_col, WS, dst_name, tag):
        s, d = self.s, self.dram
        st = contextlib.ExitStack()
        xr = [self.sbuf(f"{tag}xr{c}", [128, T], BF16, stack=st)
              for c in range(NCH)]
        srt = [self.sbuf(f"{tag}o{i}", [128, T], BF16, stack=st)
               for i in range(2)]
        self.mix_from_spill(d[src_name], [(xr, mix_col)], tag=tag)
        sr_dma = [None, None]

        def evac(dd, par, pP, mm_done, prev):
            sg = s.emit("scalar",
                        lambda e, par=par, pP=pP: e.activation(
                            srt[par][:], pP[par][:], AF.Sigmoid),
                        waits=[mm_done, sr_dma[par]], inc="sE")
            sr_dma[par] = self.dma("sync", d[dst_name][dd], srt[par][:],
                                   waits=[sg], sem="st")
            return sg

        self.proj_stream(xr, WS, evac, tag=tag)
        st.close()

    def phase_P3(self):
        s, sb, d = self.s, self.sb, self.dram
        st = contextlib.ExitStack()
        xk = [self.sbuf(f"xk{c}", [128, T], BF16, stack=st) for c in range(NCH)]
        xv = [self.sbuf(f"xv{c}", [128, T], BF16, stack=st) for c in range(NCH)]
        self.mix_from_spill(d["sp_ain"], [(xk, "tmk_c"), (xv, "tmv_c")],
                            tag="kv")
        xkv_done = s.tok("vE")
        self.new_phase()

        HB = 512
        NHB = T // HB
        kstr = [self.sbuf(f"kst{i}", [128, C], BF16, stack=st) for i in range(2)]
        vstr = [self.sbuf(f"vst{i}", [128, C], BF16, stack=st) for i in range(2)]
        pK = [self.psum(f"pK{i}", [128, HB], stack=st) for i in range(2)]
        pV = [self.psum(f"pV{i}", [128, HB], stack=st) for i in range(2)]
        ek = [self.sbuf(f"ekb{i}", [128, HB], stack=st) for i in range(2)]
        ekv = [self.sbuf(f"ekvb{i}", [128, HB], stack=st) for i in range(2)]
        nbuf = [self.sbuf(f"nbf{i}", [128, HB], stack=st) for i in range(2)]
        dbuf = [self.sbuf(f"dbf{i}", [128, HB], stack=st) for i in range(2)]
        zbf = [self.sbuf(f"zbf{i}", [128, HB], BF16, stack=st) for i in range(2)]
        srh = [self.sbuf(f"srh{i}", [128, HB], BF16, stack=st) for i in range(2)]
        sA = self.sbuf("scanA", [128, HB + 1], stack=st)
        sB = self.sbuf("scanB", [128, HB + 1], stack=st)

        strip_tok = [None, None]
        pk_free = [None, None]
        pv_free = [None, None]
        ek_cons = [None, None]
        zdma = [None, None]
        sr_used = [None, None]
        sr_ready_l = self.st_all()
        for dd in range(2):
            strip_tok[dd] = self.dma("gpsimd", kstr[dd][:], d["WkS"][dd],
                                     waits=[xkv_done])
            strip_tok[dd] = self.dma("gpsimd", vstr[dd][:], d["WvS"][dd])
        hidx = 0
        for dd in range(NCH):
            spar = dd % 2
            s.emit("vector", lambda e: e.memset(sA[:, 0:1], 0.0), inc="vE")
            s.emit("vector", lambda e: e.memset(sB[:, 0:1], 0.0), inc="vE")
            for hf in range(NHB):
                par = hidx % 2
                off = hf * HB
                mmw = [strip_tok[spar], xkv_done, pk_free[par], pv_free[par]]
                for kc in range(NCH):
                    s.emit("tensor",
                           lambda e, spar=spar, par=par, kc=kc, off=off:
                           e.matmul(pK[par][:],
                                    kstr[spar][:, kc * 128:(kc + 1) * 128],
                                    xk[kc][:, off:off + HB],
                                    start=(kc == 0), stop=(kc == NCH - 1)),
                           waits=mmw if kc == 0 else (), inc=self.cur_tE)
                for kc in range(NCH):
                    s.emit("tensor",
                           lambda e, spar=spar, par=par, kc=kc, off=off:
                           e.matmul(pV[par][:],
                                    vstr[spar][:, kc * 128:(kc + 1) * 128],
                                    xv[kc][:, off:off + HB],
                                    start=(kc == 0), stop=(kc == NCH - 1)),
                           inc=self.cur_tE)
                mm_done = s.tok(self.cur_tE)
                srl = self.dma("sync", srh[par][:],
                               d["sp_sr"][dd][:, off:off + HB],
                               waits=sr_ready_l + [sr_used[par]])
                ek_t = s.emit("scalar",
                              lambda e, par=par: e.activation(
                                  ek[par][:], pK[par][:], AF.Exp),
                              waits=[mm_done, ek_cons[par]], inc="sE")
                pk_free[par] = ek_t
                ekv_t = s.emit("vector",
                               lambda e, par=par: e.tensor_tensor(
                                   ekv[par][:], ek[par][:], pV[par][:],
                                   ALU.mult),
                               waits=[ek_t], inc="vE")
                pv_free[par] = ekv_t
                dcol = sb["d_col"][:, dd:dd + 1].broadcast_to((128, HB))
                s.emit("vector",
                       lambda e, par=par, dcol=dcol: e.tensor_tensor_scan(
                           sA[:, 1:HB + 1], dcol, ekv[par][:], sA[:, 0:1],
                           ALU.mult, ALU.add),
                       inc="vE")
                s.emit("vector",
                       lambda e, par=par, dcol=dcol: e.tensor_tensor_scan(
                           sB[:, 1:HB + 1], dcol, ek[par][:], sB[:, 0:1],
                           ALU.mult, ALU.add),
                       inc="vE")
                s.emit("vector",
                       lambda e, par=par, dd=dd: e.scalar_tensor_tensor(
                           nbuf[par][:], ekv[par][:], sb["eu_col"][:, dd:dd + 1],
                           sA[:, 0:HB], ALU.mult, ALU.add),
                       inc="vE")
                den_t = s.emit("vector",
                               lambda e, par=par, dd=dd: e.scalar_tensor_tensor(
                                   dbuf[par][:], ek[par][:],
                                   sb["eu_col"][:, dd:dd + 1], sB[:, 0:HB],
                                   ALU.mult, ALU.add),
                               inc="vE")
                ek_cons[par] = den_t
                s.emit("vector", lambda e: e.tensor_copy(sA[:, 0:1],
                                                         sA[:, HB:HB + 1]),
                       inc="vE")
                s.emit("vector", lambda e: e.tensor_copy(sB[:, 0:1],
                                                         sB[:, HB:HB + 1]),
                       inc="vE")
                s.emit("scalar",
                       lambda e, par=par: e.activation(dbuf[par][:],
                                                       dbuf[par][:], AF.Ln),
                       waits=[den_t], inc="sE")
                rec_t = s.emit("scalar",
                               lambda e, par=par: e.activation(
                                   dbuf[par][:], dbuf[par][:], AF.Exp,
                                   scale=-1.0),
                               inc="sE")
                s.emit("vector",
                       lambda e, par=par: e.tensor_tensor(
                           nbuf[par][:], nbuf[par][:], dbuf[par][:], ALU.mult),
                       waits=[rec_t], inc="vE")
                z_t = s.emit("vector",
                             lambda e, par=par: e.tensor_tensor(
                                 zbf[par][:], nbuf[par][:], srh[par][:],
                                 ALU.mult),
                             waits=[srl, zdma[par]], inc="vE")
                sr_used[par] = z_t
                zdma[par] = self.dma("sync", d["sp_z"][dd][:, off:off + HB],
                                     zbf[par][:], waits=[z_t], sem="st")
                hidx += 1
            done_d = s.tok(self.cur_tE)
            if dd + 2 < NCH:
                strip_tok[spar] = self.dma("gpsimd", kstr[spar][:],
                                           d["WkS"][dd + 2], waits=[done_d])
                strip_tok[spar] = self.dma("gpsimd", vstr[spar][:],
                                           d["WvS"][dd + 2])
        st.close()

    def phase_A3(self):
        self.new_phase()
        s, sb, d = self.s, self.sb, self.dram
        st = contextlib.ExitStack()
        z = [self.sbuf(f"z{c}", [128, T], BF16, stack=st) for c in range(NCH)]
        z_ready_l = self.st_all()
        ltoks = [self.dma("sync", z[c][:], d["sp_z"][c], waits=z_ready_l)
                 for c in range(NCH)]
        s.emit("vector", lambda e: e.engine_nop(), waits=ltoks, inc="vE")
        xrt = [self.sbuf(f"a3x{i}", [128, TB], stack=st) for i in range(2)]
        amt = [self.sbuf(f"a3m{i}", [128, TB], stack=st) for i in range(2)]
        r1bc = self.sbuf("r1bc", [128, T], stack=st)
        m0r1bc = self.sbuf("m0r1bc", [128, T], stack=st)
        mu0, r1 = sb["mu0"], sb["r1"]
        m0r1 = self.sbuf("m0r1", [1, T], stack=st)
        mr_t = s.emit("vector",
                      lambda e: e.tensor_tensor(m0r1[0:1, :], mu0[0:1, :],
                                                r1[0:1, :], ALU.mult),
                      inc="vE")
        bst = contextlib.ExitStack()
        pB = self.psum("a3bb", [128, TB], stack=bst)
        prev = None
        for (dst, row) in ((r1bc, r1), (m0r1bc, m0r1)):
            for tb in range(NTB):
                sl = slice(tb * TB, (tb + 1) * TB)
                s.emit("tensor",
                       lambda e, sl=sl, row=row: e.matmul(
                           pB[:], sb["ones_row"][:], row[0:1, sl],
                           start=True, stop=True),
                       waits=[mr_t, prev], inc=self.cur_tE)
                mmt = s.tok(self.cur_tE)
                prev = s.emit("vector",
                              lambda e, sl=sl, dst=dst: e.tensor_copy(
                                  dst[:, sl], pB[:]),
                              waits=[mmt], inc="vE")
        bst.close()
        bc_done = prev
        x_tok = [None, None]
        am_tok = [None, None]

        def evac(dd, par, pP, mm_done, prev):
            last = None
            for tb in range(NTB):
                sl = slice(tb * TB, (tb + 1) * TB)
                i = tb % 2
                lt = self.dma("sync", xrt[i][:],
                              d["xT"][dd * 128:(dd + 1) * 128, sl],
                              waits=[x_tok[i]])
                s.emit("vector",
                       lambda e, i=i, sl=sl: e.tensor_tensor(
                           amt[i][:], xrt[i][:], r1bc[:, sl], ALU.mult),
                       waits=[lt, bc_done, mm_done, am_tok[i]], inc="vE")
                x_tok[i] = s.tok("vE")
                s.emit("vector",
                       lambda e, i=i, sl=sl: e.tensor_tensor(
                           amt[i][:], amt[i][:], m0r1bc[:, sl], ALU.subtract),
                       inc="vE")
                last = s.emit("vector",
                              lambda e, i=i, par=par, sl=sl, pP=pP:
                              e.tensor_tensor(amt[i][:], amt[i][:],
                                              pP[par][:, sl], ALU.add),
                              inc="vE")
                am_tok[i] = self.dma("sync", d["sp_xc"][dd][:, sl], amt[i][:],
                                     waits=[last], sem="st")
            return last

        self.proj_stream(z, "WoS", evac, tag="a3")
        st.close()

    def phase_A4b(self):
        s, d = self.s, self.dram
        st = contextlib.ExitStack()
        ck = [self.sbuf(f"ck{c}", [128, T], BF16, stack=st) for c in range(NCH)]
        self.mix_from_spill(d["sp_n1"], [(ck, "cmk_c")], tag="ck")
        self.mlp_up(ck, "WckS", act="relusq", st=st, tag="a4")
        st.close()

    def mlp_up(self, rhs, WS, act, st, tag):
        self.new_phase()
        s, sb, d = self.s, self.sb, self.dram
        wstr = [self.sbuf(f"{tag}w{i}", [128, C], BF16, stack=st)
                for i in range(2)]
        ht = [self.sbuf(f"{tag}h{i}", [128, T], BF16, stack=st)
              for i in range(2)]
        if act == "relusq":
            rl = [self.sbuf(f"{tag}r{i}", [128, T], stack=st)
                  for i in range(2)]
        pP = [self.psum(f"{tag}p{i}", [128, T], stack=st) for i in range(2)]
        rhs_ready = s.tok("vE")
        strip_tok = [None, None]
        psum_free = [None, None]
        hdma = [None, None]
        for hc in range(2):
            strip_tok[hc] = self.dma("gpsimd", wstr[hc][:], d[WS][hc],
                                     waits=[rhs_ready])
        for hc in range(NHCH):
            par = hc % 2
            mmw = [strip_tok[par], rhs_ready, psum_free[par]]
            for tb in range(NTB):
                so = tb * TB
                for kc in range(NCH):
                    s.emit("tensor",
                           lambda e, par=par, so=so, kc=kc: e.matmul(
                               pP[par][:, so:so + TB],
                               wstr[par][:, kc * 128:(kc + 1) * 128],
                               rhs[kc][:, so:so + TB],
                               start=(kc == 0), stop=(kc == NCH - 1)),
                           waits=mmw if kc == 0 and tb == 0 else (),
                           inc=self.cur_tE)
            mm_done = s.tok(self.cur_tE)
            if act == "relusq":
                rl_t = s.emit("scalar",
                              lambda e, par=par: e.activation(
                                  rl[par][:], pP[par][:], AF.Relu),
                              waits=[mm_done], inc="sE")
                h_t = s.emit("vector",
                             lambda e, par=par: e.tensor_tensor(
                                 ht[par][:], rl[par][:], pP[par][:], ALU.mult),
                             waits=[rl_t, hdma[par]], inc="vE")
            else:
                h_t = s.emit("scalar",
                             lambda e, par=par, hc=hc: e.activation(
                                 ht[par][:], pP[par][:], AF.Gelu_apprx_tanh,
                                 bias=sb["b1_c"][:, hc:hc + 1]),
                             waits=[mm_done, hdma[par]], inc="sE")
            psum_free[par] = h_t
            hdma[par] = self.dma("sync", d["sp_h"][hc], ht[par][:],
                                 waits=[h_t], sem="st")
            if hc + 2 < NHCH:
                strip_tok[par] = self.dma("gpsimd", wstr[par][:],
                                          d[WS][hc + 2], waits=[mm_done])

    def phase_A4c2(self):
        self.mlp_down("WcvQ", tag="cv", gate_spill="sp_scr")

    def mlp_down(self, WQ, tag, gate_spill=None, gbc=None, b2col=None,
                 dst=None):
        self.new_phase()
        s, sb, d = self.s, self.sb, self.dram
        st = contextlib.ExitStack()
        KW = NHCH * 128
        wq = [self.sbuf(f"{tag}qw{i}", [128, 2 * KW], BF16, stack=st)
              for i in range(2)]
        htl = [self.sbuf(f"{tag}h{i}", [128, TB], BF16, stack=st)
               for i in range(4)]
        xct = [self.sbuf(f"{tag}x{i}", [128, TB], stack=st) for i in range(2)]
        if gate_spill is not None:
            gt = [self.sbuf(f"{tag}g{i}", [128, TB], BF16, stack=st)
                  for i in range(2)]
        tm = [self.sbuf(f"{tag}t{i}", [128, TB], stack=st) for i in range(2)]
        pP = [self.psum(f"{tag}p{i}", [128, 2 * TB], stack=st)
              for i in range(2)]
        if dst is None:
            dst = [d["sp_xc"][dd] for dd in range(NCH)]
        h_ready_l = self.st_all()
        gbc_ready = s.tok("vE")
        wq_tok = [None, None]
        wq_free = [None, None]
        ht_free = [None, None, None, None]
        xc_tok = [None, None]
        gt_tok = [None, None]
        pf = [None, None]
        wq_tok[0] = self.dma("gpsimd", wq[0][:],
                             d[WQ][0][:, 0:2 * KW], waits=h_ready_l)
        for hq in range(8):
            self.rotate_sems()
            qp = hq % 2
            if hq + 1 < 8:
                nq, half = divmod(hq + 1, 2)
                wq_tok[(hq + 1) % 2] = self.dma(
                    "gpsimd", wq[(hq + 1) % 2][:],
                    d[WQ][nq][:, half * 2 * KW:(half + 1) * 2 * KW],
                    waits=[wq_free[(hq + 1) % 2]])
            for tb in range(NTB):
                sl = slice(tb * TB, (tb + 1) * TB)
                par = (hq * NTB + tb) % 2
                for hc in range(NHCH):
                    hi = hc % 4
                    lt = self.dma("sync", htl[hi][:], d["sp_h"][hc][:, sl],
                                  waits=h_ready_l + [ht_free[hi]])
                    for j in range(2):
                        s.emit("tensor",
                               lambda e, qp=qp, par=par, j=j, hc=hc, hi=hi:
                               e.matmul(
                                   pP[par][:, j * TB:(j + 1) * TB],
                                   wq[qp][:, j * KW + hc * 128:
                                          j * KW + (hc + 1) * 128],
                                   htl[hi][:], start=(hc == 0),
                                   stop=(hc == NHCH - 1)),
                               waits=([wq_tok[qp], lt, pf[par]]
                                      if hc == 0 and j == 0 else [lt]),
                               inc=self.cur_tE)
                        if j == 1:
                            ht_free[hi] = s.tok(self.cur_tE)
                mm_done = s.tok(self.cur_tE)
                last = None
                for j in range(2):
                    dd = hq * 2 + j
                    i = j % 2
                    xl = self.dma("sync", xct[i][:], d["sp_xc"][dd][:, sl],
                                  waits=h_ready_l + [xc_tok[i]])
                    if gate_spill is not None:
                        gl = self.dma("sync", gt[i][:],
                                      d[gate_spill][dd][:, sl],
                                      waits=h_ready_l + [gt_tok[i]])
                        s.emit("vector",
                               lambda e, i=i, j=j, par=par: e.tensor_tensor(
                                   tm[i][:], pP[par][:, j * TB:(j + 1) * TB],
                                   gt[i][:], ALU.mult),
                               waits=[mm_done, gl], inc="vE")
                        gt_tok[i] = s.tok("vE")
                    else:
                        s.emit("vector",
                               lambda e, i=i, j=j, par=par, dd=dd, sl=sl:
                               e.scalar_tensor_tensor(
                                   tm[i][:], pP[par][:, j * TB:(j + 1) * TB],
                                   b2col[:, dd:dd + 1], gbc[:, sl],
                                   ALU.add, ALU.mult),
                               waits=[mm_done, gbc_ready], inc="vE")
                    last = s.emit("vector",
                                  lambda e, i=i: e.tensor_tensor(
                                      xct[i][:], xct[i][:], tm[i][:], ALU.add),
                                  waits=[xl], inc="vE")
                    xc_tok[i] = self.dma("sync", dst[dd][:, sl],
                                         xct[i][:], waits=[last], sem="st")
                pf[par] = last
            wq_free[qp] = s.tok(self.cur_tE)
        st.close()

    def phase_A4d(self):
        self.new_phase()
        s, d = self.s, self.dram
        st = contextlib.ExitStack()
        xct = [self.sbuf(f"adx{i}", [128, TB], stack=st) for i in range(2)]
        x0t = [self.sbuf(f"ad0{i}", [128, TB], stack=st) for i in range(2)]
        ready_l = self.st_all()
        use = [None, None]
        for c in range(NCH):
            for tb in range(NTB):
                sl = slice(tb * TB, (tb + 1) * TB)
                i = (c * NTB + tb) % 2
                l1 = self.dma("sync", xct[i][:], d["sp_xc"][c][:, sl],
                              waits=ready_l + [use[i]])
                l2 = self.dma("sync", x0t[i][:],
                              d["xT"][c * 128:(c + 1) * 128, sl])
                t = s.emit("vector",
                           lambda e, i=i: e.tensor_tensor(
                               xct[i][:], xct[i][:], x0t[i][:], ALU.add),
                           waits=[l1, l2], inc="vE")
                use[i] = self.dma("sync", d["sp_xc"][c][:, sl], xct[i][:],
                                  waits=[t], sem="st")
        st.close()



def build_nc(upto="OUT", dbg=()):
    nc = bass.Bass("TRN2", target_bir_lowering=False)
    stack = contextlib.ExitStack()
    with stack:
        b = Builder(nc, stack, dbg=dbg)
        with nc.Block() as block:
            b.build_all(upto=upto)
            b.fence()
            b.s.finalize(block)
    return nc, b


def kernel(**inputs):
    per_core = stage_inputs(inputs)
    nc, b = build_nc(upto="OUT")
    res = run_bass_kernel_spmd(nc, per_core, core_ids=list(range(NCORES)))
    out = np.empty((NCORES, T, C), dtype=np.float32)
    for i in range(NCORES):
        out[i] = res.results[i]["yT"].reshape(C, T).T
    return out
